# version 06
# speedup vs baseline: 1.0622x; 1.0049x over previous
import contextlib
import numpy as np
import concourse.bass as bass
import concourse.mybir as mybir
from concourse.bass_utils import run_bass_kernel_spmd

F32 = mybir.dt.float32
BF16 = mybir.dt.bfloat16
I8 = mybir.dt.int8
AF = mybir.ActivationFunctionType
ALU = mybir.AluOpType
AX = mybir.AxisListType

NCORES = 8
D = 1024
H = 4096
DIN = 2048
DSTATE = 128
NH = 32
HD = 64
CONVD = 2304
DPROJ = 4384
T_OWN = 4096
HALO = 128
T_SLAB = T_OWN + HALO
NT = T_SLAB // 128
NT_OWN = T_OWN // 128
EPS = 1e-5
NEGBIG = -1e9
CHUNK = 128

PHASES = ("P0", "F1", "M1", "M2", "M3", "F3")
import os as _os
M1STAGE = int(_os.environ.get("M1STAGE", "9"))


class Ctx:
    def __init__(self, nc, es, blk):
        self.nc = nc
        self.es = es
        self.blk = blk
        self.sem = {}
        self.cnt = {}
        for name in ("sV", "sS", "sG", "sT", "sD", "sE", "sQ", "sC"):
            self.sem[name] = es.enter_context(nc.semaphore(name))
            self.cnt[name] = 0

    def inc(self, inst, s, k=1):
        inst.then_inc(self.sem[s], k)
        self.cnt[s] += k
        return self.cnt[s]

    def w(self, eng, s, val=None):
        eng.wait_ge(self.sem[s], self.cnt[s] if val is None else val)

    def barrier(self):
        base = self.cnt["sQ"]
        target = base + 5
        for en in ("sync", "tensor", "vector", "scalar", "gpsimd"):
            dec = getattr(self.blk, en)

            @dec
            def _(e, en=en):
                if en == "sync":
                    e.wait_ge(self.sem["sD"], self.cnt["sD"])
                if en == "gpsimd":
                    e.wait_ge(self.sem["sE"], self.cnt["sE"])
                e.nop().then_inc(self.sem["sQ"], 1)
                e.wait_ge(self.sem["sQ"], target)
        self.cnt["sQ"] = target


def build_nc(debug_taps=()):
    nc = bass.Bass("TRN2", target_bir_lowering=False,
                   detect_race_conditions=False)

    def inp(name, shape, dt=F32):
        return nc.dram_tensor(name, shape, dt, kind="ExternalInput").ap()

    x0 = inp("x0", [T_SLAB, D])
    w1t = inp("w1t", [D, H])
    w2t = inp("w2t", [H, D])
    inpj = inp("inpj", [D, DPROJ])
    outpj = inp("outpj", [DIN, D])
    ln1g_r = inp("ln1g_r", [128, D])
    ln1b_r = inp("ln1b_r", [128, D])
    ln2g_r = inp("ln2g_r", [128, H])
    ln2b_r = inp("ln2b_r", [128, H])
    b1_r = inp("b1_r", [128, H])
    b2_r = inp("b2_r", [128, D])
    nrmw_r = inp("nrmw_r", [128, D])
    gatew_r = inp("gatew_r", [128, DIN])
    dtbias_r = inp("dtbias_r", [128, NH])
    alog_r = inp("alog_r", [128, NH])
    dskip_r = inp("dskip_r", [128, NH])
    convw_r = inp("convw_r", [128, 18 * 4])
    convb_r = inp("convb_r", [128, 18])
    hm_r = inp("hm_r", [128, 1])
    mask_r = inp("mask_r", [128, 8])
    sel_r = inp("sel_r", [128, 64])

    x3_out = nc.dram_tensor("x3", [T_OWN, D], F32, kind="ExternalOutput").ap()

    def scratch(name, shape, dt=F32):
        kind = "ExternalOutput" if name in debug_taps else "Internal"
        return nc.dram_tensor(name, shape, dt, kind=kind).ap()

    x1_hbm = scratch("x1_hbm", [T_SLAB, D])
    x2_hbm = scratch("x2_hbm", [T_OWN, D])
    pln_hbm = scratch("pln_hbm", [T_SLAB, D])
    z_hbm = scratch("z_hbm", [T_OWN, DIN])
    q2_hbm = scratch("q2_hbm", [T_SLAB, H], BF16)
    qw1_hbm = scratch("qw1_hbm", [D, H], BF16)
    qw2_hbm = scratch("qw2_hbm", [H, D], BF16)
    xbc_hbm = scratch("xbc_hbm", [CONVD, T_OWN], BF16)
    y_hbm = scratch("y_hbm", [T_OWN, DIN])
    h1dbg = scratch("h1dbg", [T_SLAB, D])
    q1dbg = scratch("q1dbg", [T_SLAB, D], BF16)
    h3dbg = scratch("h3dbg", [T_SLAB, H])
    ydbg = scratch("ydbg", [T_OWN, DIN])
    rdbg = scratch("rdbg", [128, DIN])
    m2dbg = scratch("m2dbg", [128, 512])
    yndbg = scratch("yndbg", [T_OWN, DIN], BF16)
    h4dbg = scratch("h4dbg", [T_SLAB, H])
    acum_hbm = scratch("acum_hbm", [NH, T_OWN])
    arow_hbm = scratch("arow_hbm", [4, NT_OWN * NH])
    cc_in = nc.dram_tensor("cc_in", [128, DIN + NH], F32, kind="Internal").ap()
    cc_out = nc.dram_tensor("cc_out", [NCORES * 128, DIN + NH], F32,
                            kind="Internal").ap()

    with contextlib.ExitStack() as es:
        blk = es.enter_context(nc.Block())
        C = Ctx(nc, es, blk)
        sb = lambda name, shape, dt=F32: es.enter_context(
            nc.sbuf_tensor(name, shape, dt))

        ident = sb("ident", [128, 128])
        ident_bf = sb("ident_bf", [128, 128], BF16)
        trineg = sb("trineg", [128, 128])
        ones1 = sb("ones1", [1, 128])
        ones_col = sb("ones_col", [128, 1])
        mw_cols = sb("mw_cols", [128, 2])
        swrec = sb("swrec", [128, 2])
        inv2_cols = sb("inv2_cols", [128, NT])
        r2_cols = sb("r2_cols", [128, NT])

        cst = dict(ident=ident, ident_bf=ident_bf, trineg=trineg, ones1=ones1,
                   ones_col=ones_col, mw_cols=mw_cols, swrec=swrec)

        CONST_VALS = [EPS, -1.0, 127.0, 1.0 / 127.0, 0.5, -0.5, 1.5, -1e-5,
                      1.0 / D, 1.0 / H, 1.0 / DIN, 1.0 / (D * H),
                      2.0, -2.0, 1e-5]
        CONST_VALS = sorted(set(float(v) for v in CONST_VALS))
        cpool = sb("cpool", [128, len(CONST_VALS)])
        for j, v in enumerate(CONST_VALS):
            nc.const_aps.register(F32, v, cpool[:, j:j + 1])

        @blk.gpsimd
        def _(g):
            for j, v in enumerate(CONST_VALS):
                g.memset(cpool[:, j:j + 1], v)
            C.inc(g.nop(), "sG", 1)
        nconst_g = C.cnt["sG"]
        for en in ("sync", "tensor", "vector", "scalar"):
            dec = getattr(blk, en)

            @dec
            def _(e):
                e.wait_ge(C.sem["sG"], nconst_g)

        if "P0" in PHASES:
            phase_p0(nc, blk, C, cst, w1t, w2t, qw1_hbm, qw2_hbm)
            C.barrier()

        if "F1" in PHASES:
            ffn_phase(nc, blk, C, cst, x_in=x0, res_in=x0, x_out=x1_hbm,
                      n_tiles=NT, half_out=True, qw1_hbm=qw1_hbm,
                      qw2_hbm=qw2_hbm, q2_hbm=q2_hbm, ln1g_r=ln1g_r,
                      ln1b_r=ln1b_r, ln2g_r=ln2g_r, ln2b_r=ln2b_r,
                      b1_r=b1_r, b2_r=b2_r, inv2_cols=inv2_cols,
                      r2_cols=r2_cols, x_final=None,
                      dbg=any(t in debug_taps for t in ("h1dbg", "q1dbg", "h3dbg", "h4dbg")), h1dbg_t=h1dbg,
                      h4dbg_t=h4dbg, q1dbg_t=q1dbg, h3dbg_t=h3dbg)
            C.barrier()

        mes = contextlib.ExitStack()
        if "M1" in PHASES or "M2" in PHASES or "M3" in PHASES:
            mes.__enter__()
            msb = lambda name, shape, dt=F32: mes.enter_context(
                nc.sbuf_tensor(name, shape, dt))
            dt_tm = msb("dt_tm", [128, NT_OWN * NH])
            dtd_tm = msb("dtd_tm", [128, NT_OWN * NH])
            expa_tm = msb("expa_tm", [128, NT_OWN * NH])
            eacd_tm = msb("eacd_tm", [128, NT_OWN * NH])
            alast_cm = msb("alast_cm", [NH, NT_OWN])
            drep = msb("drep", [128, NT_OWN * NH])
            cdrep = msb("cdrep", [128, NT_OWN * NH])
            p_state = msb("p_state", [128, DIN])
            r_state = msb("r_state", [128, DIN])
            dk_rep = msb("dk_rep", [128, NH])

        if "M1" in PHASES:
            m1_phase(nc, blk, C, cst, x1_hbm=x1_hbm, pln_hbm=pln_hbm,
                     z_hbm=z_hbm, xbc_hbm=xbc_hbm, acum_hbm=acum_hbm,
                     arow_hbm=arow_hbm, inpj=inpj, nrmw_r=nrmw_r,
                     dtbias_r=dtbias_r, alog_r=alog_r, convw_r=convw_r,
                     convb_r=convb_r, hm_r=hm_r,
                     dt_tm=dt_tm, dtd_tm=dtd_tm, expa_tm=expa_tm,
                     eacd_tm=eacd_tm, alast_cm=alast_cm, drep=drep,
                     cdrep=cdrep, dk_rep=dk_rep)
            C.barrier()

        if "M2" in PHASES:
            m2_phase(nc, blk, C, cst, xbc_hbm=xbc_hbm, acum_hbm=acum_hbm,
                     y_hbm=y_hbm, dt_tm=dt_tm, dtd_tm=dtd_tm, expa_tm=expa_tm,
                     drep=drep, dskip_r=dskip_r,
                     p_state=p_state, cc_in=cc_in, cc_out=cc_out,
                     dk_rep=dk_rep, mask_r=mask_r, sel_r=sel_r,
                     r_state=r_state,
                     m2dbg_t=(m2dbg if "m2dbg" in debug_taps else None),
                     rdbg2_t=(rdbg if "m2dbg" in debug_taps else None))
            C.barrier()

        if "M3" in PHASES:
            m3_phase(nc, blk, C, cst, y_hbm=y_hbm, z_hbm=z_hbm, x1_hbm=x1_hbm,
                     pln_hbm=pln_hbm, x2_hbm=x2_hbm, xbc_hbm=xbc_hbm,
                     outpj=outpj, gatew_r=gatew_r, eacd_tm=eacd_tm,
                     expa_tm=expa_tm, cdrep=cdrep, r_state=r_state,
                     dbg=("ydbg" in debug_taps), ydbg_t=ydbg, yndbg_t=yndbg,
                     rdbg_t=rdbg)
            C.barrier()

        if "M1" in PHASES or "M2" in PHASES or "M3" in PHASES:
            mes.__exit__(None, None, None)

        if "F3" in PHASES:
            ffn_phase(nc, blk, C, cst, x_in=x2_hbm, res_in=x2_hbm,
                      x_out=None, n_tiles=NT_OWN, half_out=True,
                      qw1_hbm=qw1_hbm, qw2_hbm=qw2_hbm, q2_hbm=q2_hbm,
                      ln1g_r=ln1g_r, ln1b_r=ln1b_r, ln2g_r=ln2g_r,
                      ln2b_r=ln2b_r, b1_r=b1_r, b2_r=b2_r,
                      inv2_cols=inv2_cols, r2_cols=r2_cols, x_final=x3_out)
            C.barrier()
        elif "F1" in PHASES and "M1" not in PHASES:
            copy_hbm(nc, blk, C, x1_hbm, x3_out, HALO, T_OWN)
            C.barrier()
        elif "M3" in PHASES and "F3" not in PHASES:
            copy_hbm(nc, blk, C, x2_hbm, x3_out, 0, T_OWN)
            C.barrier()

    return nc


def copy_hbm(nc, blk, C, src, dst, src_off, rows):
    with contextlib.ExitStack() as pes:
        tb = pes.enter_context(nc.sbuf_tensor("cp_t", [128, D], F32))
        for i in range(rows // 128):
            @blk.sync
            def _(sy, i=i):
                nld = C.inc(sy.dma_start(
                    out=tb[:],
                    in_=src[src_off + i * 128:src_off + (i + 1) * 128, :]),
                    "sD", 16)
                sy.wait_ge(C.sem["sD"], nld)
                nst = C.inc(sy.dma_start(
                    out=dst[i * 128:(i + 1) * 128, :], in_=tb[:]), "sD", 16)
                sy.wait_ge(C.sem["sD"], nst)


def phase_p0(nc, blk, C, cst, w1t, w2t, qw1_hbm, qw2_hbm):
    with contextlib.ExitStack() as pes:
        psb = lambda name, shape, dt=F32: pes.enter_context(
            nc.sbuf_tensor(name, shape, dt))
        pps = lambda name, shape: pes.enter_context(
            nc.psum_tensor(name, shape, F32))
        iota_t = psb("iota_t", [128, 128])
        wrow = psb("wrow", [128, H])
        wq_i8 = psb("wq_i8", [128, H], I8)
        wq_bf = psb("wq_bf", [128, H], BF16)
        stat = psb("p0stat", [128, 40])
        tot_ps = pps("tot_ps", [1, 2])
        bc_ps = pps("bc_ps", [128, 2])

        @blk.gpsimd
        def _(g):
            g.memset(cst["ones1"][:], 1.0)
            g.memset(cst["ones_col"][:], 1.0)
            g.iota(iota_t[:], pattern=[[-1, 128]], base=0,
                   channel_multiplier=1, allow_small_or_imprecise_dtypes=True)
            g.tensor_single_scalar(cst["ident"][:], iota_t[:], 0.0,
                                   ALU.is_equal)
            g.tensor_copy(cst["ident_bf"][:], cst["ident"][:])
            g.iota(iota_t[:], pattern=[[1, 128]], base=0,
                   channel_multiplier=-1, allow_small_or_imprecise_dtypes=True)
            g.tensor_single_scalar(iota_t[:], iota_t[:], 0.0, ALU.is_lt)
            C.inc(g.tensor_scalar_mul(cst["trineg"][:], iota_t[:],
                                      float(NEGBIG)), "sG", 1)

        for idx, (w_in, wq_out, rows, cols) in enumerate(
                ((w1t, qw1_hbm, D, H), (w2t, qw2_hbm, H, D))):
            nrt = rows // 128
            for r in range(nrt):
                @blk.sync
                def _(sy, r=r, w_in=w_in, cols=cols):
                    if r >= 1:
                        C.w(sy, "sV")
                    C.inc(sy.dma_start(out=wrow[:, 0:cols],
                                       in_=w_in[r * 128:(r + 1) * 128, :]),
                          "sD", 16)

                @blk.vector
                def _(v, r=r, cols=cols):
                    C.w(v, "sD")
                    C.inc(v.reduce_sum(stat[:, r:r + 1], wrow[:, 0:cols],
                                       axis=AX.X, apply_absolute_value=True),
                          "sV", 1)

            @blk.scalar
            def _(s, nrt=nrt):
                C.w(s, "sV")
                s.copy(stat[:, 34:35], stat[:, 0:1])
                last = None
                for r in range(1, nrt):
                    last = s.add(stat[:, 34:35], stat[:, 34:35],
                                 stat[:, r:r + 1])
                C.inc(last if last is not None else
                      s.copy(stat[:, 34:35], stat[:, 34:35]), "sS", 1)

            @blk.tensor
            def _(t, idx=idx):
                C.w(t, "sS")
                C.w(t, "sG", 1)
                C.inc(t.matmul(tot_ps[0:1, idx:idx + 1], cst["ones_col"][:],
                               stat[:, 34:35], start=True, stop=True), "sT", 1)

            @blk.scalar
            def _(s, idx=idx, rows=rows, cols=cols):
                C.w(s, "sT")
                C.inc(s.mul(stat[0:1, 35:36], tot_ps[0:1, idx:idx + 1],
                            1.0 / (rows * cols)), "sS", 1)

            @blk.gpsimd
            def _(g):
                C.w(g, "sS")
                C.inc(g.tensor_scalar_max(stat[0:1, 35:36], stat[0:1, 35:36],
                                          1e-5), "sG", 1)

            @blk.tensor
            def _(t, idx=idx):
                C.w(t, "sG")
                C.inc(t.matmul(bc_ps[:, idx:idx + 1], cst["ones1"][:],
                               stat[0:1, 35:36], start=True, stop=True),
                      "sT", 1)

            @blk.scalar
            def _(s, idx=idx):
                C.w(s, "sT")
                C.inc(s.copy(cst["mw_cols"][:, idx:idx + 1],
                             bc_ps[:, idx:idx + 1]), "sS", 1)

            @blk.vector
            def _(v, idx=idx):
                C.w(v, "sS")
                C.inc(v.reciprocal(cst["swrec"][:, idx:idx + 1],
                                   cst["mw_cols"][:, idx:idx + 1]), "sV", 1)

            for r in range(nrt):
                @blk.sync
                def _(sy, r=r, w_in=w_in, cols=cols):
                    C.w(sy, "sV")
                    if r >= 1:
                        C.w(sy, "sS")
                        C.w(sy, "sD")
                    C.inc(sy.dma_start(out=wrow[:, 0:cols],
                                       in_=w_in[r * 128:(r + 1) * 128, :]),
                          "sD", 16)

                @blk.scalar
                def _(s, r=r, idx=idx, cols=cols):
                    C.w(s, "sD")
                    C.w(s, "sV")
                    C.inc(s.activation(wq_i8[:, 0:cols], wrow[:, 0:cols],
                                       AF.Copy, bias=0.0,
                                       scale=cst["swrec"][:, idx:idx + 1]),
                          "sS", 1)

                @blk.vector
                def _(v, r=r, cols=cols):
                    C.w(v, "sS")
                    if r >= 1:
                        C.w(v, "sD")
                    C.inc(v.tensor_scalar(wq_bf[:, 0:cols], wq_i8[:, 0:cols],
                                          -1.0, 1.0, ALU.max, ALU.min),
                          "sV", 1)

                @blk.sync
                def _(sy, r=r, wq_out=wq_out, cols=cols):
                    C.w(sy, "sV")
                    C.inc(sy.dma_start(out=wq_out[r * 128:(r + 1) * 128, :],
                                       in_=wq_bf[:, 0:cols]), "sD", 16)




def emit_rsqrt(blk, C, a_col, out_col, t_col):
    @blk.scalar
    def _(s):
        C.inc(s.sqrt(out_col, a_col), "sS", 1)

    @blk.vector
    def _(v):
        C.w(v, "sS")
        C.inc(v.reciprocal(t_col, out_col), "sV", 1)

    @blk.scalar
    def _(s):
        C.w(s, "sV")
        s.mul(out_col, t_col, a_col)
        s.mul(out_col, out_col, t_col)
        s.mul(out_col, out_col, -0.5)
        s.add(out_col, out_col, 1.5)
        s.mul(out_col, out_col, t_col)
        s.mul(t_col, t_col, 1.0)
        C.inc(s.mul(t_col, t_col, 1.0), "sS", 1)


_FFN_SEQ = [0]


def ffn_phase(nc, blk, C, cst, *, x_in, res_in, x_out, n_tiles, half_out,
              qw1_hbm, qw2_hbm, q2_hbm, ln1g_r, ln1b_r, ln2g_r, ln2b_r,
              b1_r, b2_r, inv2_cols, r2_cols, x_final, dbg=False,
              h1dbg_t=None, h4dbg_t=None, q1dbg_t=None, h3dbg_t=None):
    _FFN_SEQ[0] += 1
    _sfx = f"_f{_FFN_SEQ[0]}" 
    """x_out/x_final = res_in + 0.5*FFN(pLN(x_in)).

    Pass A: per 128-token tile: LN1(pLN)+quant -> mm1(+bias fold) -> gelu ->
            LN2+quant -> spill q2 (bf16).
    Pass B: per 4-tile group: transpose-read q2, mm2(+bias fold), +residual.
    """
    ld_consumed = {}
    trans_done = {}
    q2st_done = {}
    mm2_done = {}
    evb_done = {}
    xst_done = {}
    ld_consumed_set = ld_consumed.__setitem__
    trans_done_set = trans_done.__setitem__
    q2st_done_set = q2st_done.__setitem__
    mm2_done_set = mm2_done.__setitem__
    evb_done_set = evb_done.__setitem__
    xst_done_set = xst_done.__setitem__
    with contextlib.ExitStack() as pes:
        psb = lambda name, shape, dt=F32: pes.enter_context(
            nc.sbuf_tensor(name + _sfx, shape, dt))
        pps = lambda name, shape: pes.enter_context(
            nc.psum_tensor(name + _sfx, shape, F32))
        qw1_sb = psb("qw1_sb", [128, 8 * H], BF16)
        ln1g_sb = psb("ln1g_sb", [128, D])
        ln1b_sb = psb("ln1b_sb", [128, D])
        ln2g_sb = psb("ln2g_sb", [128, H])
        ln2b_sb = psb("ln2b_sb", [128, H])
        b1_sb = psb("b1_sb", [128, H], BF16)
        xb = [psb(f"ffa_x{j}", [128, D]) for j in range(2)]
        hs2 = [psb(f"ffa_hs{j}", [128, D]) for j in range(2)]
        q1i82 = [psb(f"ffa_q1i8{j}", [128, D], I8) for j in range(2)]
        q1bf2 = [psb(f"ffa_q1bf{j}", [128, D], BF16) for j in range(2)]
        q1fm2 = [psb(f"ffa_q1fm{j}", [128, 8 * 128], BF16) for j in range(2)]
        hbig = psb("ffa_hbig", [128, H])
        sqscr = psb("ffa_sqscr", [128, D])
        q2i8 = psb("ffa_q2i8", [128, H], I8)
        q2bf = psb("ffa_q2bf", [128, H], BF16)
        diag12 = [psb(f"ffa_diag1{j}", [128, 128], BF16) for j in range(2)]
        stc2 = [psb(f"ffa_stc{j}", [128, 32]) for j in range(2)]
        mm_ps = pps("ffa_mmps", [128, H])

        @blk.sync
        def _(sy):
            for k in range(8):
                C.inc(sy.dma_start(out=qw1_sb[:, k * H:(k + 1) * H],
                                   in_=qw1_hbm[k * 128:(k + 1) * 128, :]),
                      "sD", 16)
            C.inc(sy.dma_start(out=ln1g_sb[:], in_=ln1g_r), "sD", 16)
            C.inc(sy.dma_start(out=ln1b_sb[:], in_=ln1b_r), "sD", 16)
            C.inc(sy.dma_start(out=ln2g_sb[:], in_=ln2g_r), "sD", 16)
            C.inc(sy.dma_start(out=ln2b_sb[:], in_=ln2b_r), "sD", 16)

        @blk.gpsimd
        def _(g):
            C.inc(g.dma_start(out=b1_sb[:], in_=b1_r), "sE", 16)
        nw1 = C.cnt["sD"]
        nw1e = C.cnt["sE"]

        (SQ, NM, SP, AM, AMC, SA, INV1, R1, T0, T1, GS0, GS1, SP2, NM2, AM2,
         T2) = range(16)

        q2act_done = {}
        sq2_done = {}
        mm1_done = {}
        for i in range(n_tiles):
            bx = xb[i % 2]
            hs = hs2[i % 2]
            q1i8 = q1i82[i % 2]
            q1bf = q1bf2[i % 2]
            q1fm = q1fm2[i % 2]
            diag1 = diag12[i % 2]
            stc = stc2[i % 2]

            @blk.sync
            def _(sy, i=i, bx=bx):
                if i >= 2:
                    C.w(sy, "sS", ld_consumed[i - 2])
                C.inc(sy.dma_start(out=bx[:],
                                   in_=x_in[i * 128:(i + 1) * 128, :]),
                      "sD", 16)
            nld = C.cnt["sD"]

            @blk.vector
            def _(v, i=i, bx=bx, hs=hs, stc=stc):
                v.wait_ge(C.sem["sD"], nld)
                if i >= 2:
                    v.wait_ge(C.sem["sS"], q2act_done[i - 2])
                C.inc(v.reduce_sum(stc[:, SQ + 1:SQ + 2], bx[:], axis=AX.X),
                      "sV", 1)

            @blk.scalar
            def _(s, i=i, bx=bx, stc=stc):
                C.w(s, "sV")
                C.inc(s.activation(sqscr[:], bx[:], AF.Square, bias=0.0,
                                   scale=1.0, accum_out=stc[:, SQ:SQ + 1]),
                      "sS", 1)

            @blk.scalar
            def _(s, i=i):
                C.w(s, "sV")
                s.mul(stc[:, NM:NM + 1], stc[:, SQ + 1:SQ + 2], 1.0 / D)
                s.mul(stc[:, T0:T0 + 1], stc[:, SQ:SQ + 1], 1.0 / D)
                s.mul(stc[:, T1:T1 + 1], stc[:, NM:NM + 1], -1.0)
                s.mul(stc[:, T1:T1 + 1], stc[:, T1:T1 + 1],
                      stc[:, NM:NM + 1])
                s.add(stc[:, T0:T0 + 1], stc[:, T1:T1 + 1],
                      stc[:, T0:T0 + 1])
                C.inc(s.add(stc[:, T1:T1 + 1], stc[:, T0:T0 + 1], EPS),
                      "sS", 1)

            emit_rsqrt(blk, C, stc[:, T1:T1 + 1], stc[:, SP:SP + 1],
                       stc[:, T2:T2 + 1])

            @blk.scalar
            def _(s):
                s.mul(stc[:, T1:T1 + 1], stc[:, T0:T0 + 1],
                      stc[:, SP:SP + 1])
                s.mul(stc[:, T1:T1 + 1], stc[:, T1:T1 + 1],
                      stc[:, SP:SP + 1])
                C.inc(s.add(stc[:, T1:T1 + 1], stc[:, T1:T1 + 1], EPS),
                      "sS", 1)

            emit_rsqrt(blk, C, stc[:, T1:T1 + 1], stc[:, T0:T0 + 1],
                       stc[:, T2:T2 + 1])

            @blk.scalar
            def _(s, bx=bx):
                s.mul(stc[:, SP:SP + 1], stc[:, T0:T0 + 1],
                      stc[:, SP:SP + 1])
                s.mul(stc[:, NM:NM + 1], stc[:, NM:NM + 1], -1.0)
                s.mul(stc[:, NM:NM + 1], stc[:, NM:NM + 1],
                      stc[:, SP:SP + 1])
                s.activation(hs[:], bx[:], AF.Copy, bias=0.0,
                             scale=stc[:, SP:SP + 1])
                C.inc(s.add(hs[:], hs[:], stc[:, NM:NM + 1]), "sS", 1)
            ld_consumed_set(i, C.cnt["sS"])

            @blk.vector
            def _(v, ln1g_r=ln1g_r):
                C.w(v, "sS")
                C.inc(v.scalar_tensor_tensor(hs[:], hs[:], 1.0, ln1g_sb[:],
                                             ALU.mult, ALU.mult), "sV", 1)

            @blk.gpsimd
            def _(g, ln1b_r=ln1b_r):
                C.w(g, "sV")
                C.inc(g.tensor_add(hs[:], hs[:], ln1b_sb[:]), "sG", 1)

            @blk.vector
            def _(v):
                C.w(v, "sG")
                C.inc(v.reduce_max(stc[:, AM:AM + 1], hs[:], axis=AX.X,
                                   apply_absolute_value=True), "sV", 1)

            @blk.scalar
            def _(s, stc=stc):
                C.w(s, "sV")
                s.activation(stc[:, AMC:AMC + 1], stc[:, AM:AM + 1], AF.Relu,
                             bias=-1e-5, scale=1.0)
                C.inc(s.add(stc[:, AMC:AMC + 1], stc[:, AMC:AMC + 1], 1e-5),
                      "sS", 1)

            @blk.vector
            def _(v):
                C.w(v, "sS")
                C.inc(v.reciprocal(stc[:, T0:T0 + 1], stc[:, AMC:AMC + 1]),
                      "sV", 1)

            @blk.scalar
            def _(s, i=i, stc=stc):
                C.w(s, "sV")
                s.mul(stc[:, SA:SA + 1], stc[:, T0:T0 + 1], 127.0)
                s.mul(stc[:, T1:T1 + 1], stc[:, AMC:AMC + 1],
                      cst["mw_cols"][:, 0:1])
                s.mul(stc[:, INV1:INV1 + 1], stc[:, T1:T1 + 1], 1.0 / 127.0)
                C.inc(s.mul(stc[:, R1:R1 + 1], stc[:, SA:SA + 1],
                            cst["swrec"][:, 0:1]), "sS", 1)

            if dbg:
                @blk.sync
                def _(sy, i=i):
                    C.w(sy, "sS")
                    C.inc(sy.dma_start(out=h1dbg_t[i * 128:(i + 1) * 128, :],
                                       in_=hs[:]), "sD", 16)

            @blk.scalar
            def _(s):
                C.inc(s.activation(q1i8[:], hs[:], AF.Copy, bias=0.0,
                                   scale=stc[:, SA:SA + 1]), "sS", 1)

            @blk.vector
            def _(v, i=i):
                C.w(v, "sS")
                if i >= 1:
                    v.wait_ge(C.sem["sD"], trans_done[i - 1])
                v.tensor_copy(q1bf[:], q1i8[:])
                C.inc(v.tensor_scalar(diag1[:], cst["ident_bf"][:],
                                      stc[:, R1:R1 + 1], None, ALU.mult),
                      "sV", 1)

            if dbg:
                @blk.sync
                def _(sy, i=i):
                    C.w(sy, "sV")
                    C.inc(sy.dma_start(out=q1dbg_t[i * 128:(i + 1) * 128, :],
                                       in_=q1bf[:]), "sD", 16)

            @blk.sync
            def _(sy, i=i, q1fm=q1fm, q1bf=q1bf):
                C.w(sy, "sV")
                if i >= 2:
                    C.w(sy, "sT", mm1_done[i - 2])
                for k in range(8):
                    C.inc(sy.dma_start_transpose(
                        q1fm[:, k * 128:(k + 1) * 128],
                        q1bf[:, k * 128:(k + 1) * 128]), "sD", 16)
            trans_done_set(i, C.cnt["sD"])

            @blk.tensor
            def _(t, i=i, q1fm=q1fm, diag1=diag1, stc=stc):
                C.w(t, "sD")
                if i == 0:
                    t.wait_ge(C.sem["sD"], nw1)
                    t.wait_ge(C.sem["sE"], nw1e)
                if i >= 1:
                    t.wait_ge(C.sem["sS"], sq2_done[i - 1])
                t.ldweights(diag1[:])
                for b in range(8):
                    t.matmul(mm_ps[:, b * 512:(b + 1) * 512], diag1[:],
                             b1_sb[:, b * 512:(b + 1) * 512],
                             start=True, stop=False)
                for k in range(8):
                    t.ldweights(q1fm[:, k * 128:(k + 1) * 128])
                    for b in range(8):
                        ii = t.matmul(
                            mm_ps[:, b * 512:(b + 1) * 512],
                            q1fm[:, k * 128:(k + 1) * 128],
                            qw1_sb[:, k * H + b * 512:k * H + (b + 1) * 512],
                            start=False, stop=(k == 7))
                C.inc(ii, "sT", 1)
            mm1_done[i] = C.cnt["sT"]

            @blk.scalar
            def _(s, i=i, stc=stc):
                C.w(s, "sT")
                s.activation(hbig[:, 0:2048], mm_ps[:, 0:2048], AF.Gelu,
                             bias=0.0, scale=stc[:, INV1:INV1 + 1],
                             accum_out=stc[:, GS0:GS0 + 1])
                C.inc(s.activation(hbig[:, 2048:4096], mm_ps[:, 2048:4096],
                                   AF.Gelu, bias=0.0,
                                   scale=stc[:, INV1:INV1 + 1],
                                   accum_out=stc[:, GS1:GS1 + 1]), "sS", 1)

            if dbg:
                @blk.sync
                def _(sy, i=i):
                    C.w(sy, "sS")
                    C.inc(sy.dma_start(out=h3dbg_t[i * 128:(i + 1) * 128, :],
                                       in_=hbig[:]), "sD", 16)

            @blk.scalar
            def _(s, stc=stc, i=i):
                C.inc(s.activation(mm_ps[:, 0:H], hbig[:], AF.Square,
                                   bias=0.0, scale=1.0,
                                   accum_out=stc[:, SQ:SQ + 1]), "sS", 1)
            sq2_done[i] = C.cnt["sS"]

            @blk.scalar
            def _(s):
                s.add(stc[:, T0:T0 + 1], stc[:, GS1:GS1 + 1],
                      stc[:, GS0:GS0 + 1])
                s.mul(stc[:, NM2:NM2 + 1], stc[:, T0:T0 + 1], 1.0 / H)
                s.mul(stc[:, T0:T0 + 1], stc[:, SQ:SQ + 1], 1.0 / H)
                s.mul(stc[:, T1:T1 + 1], stc[:, NM2:NM2 + 1], -1.0)
                s.mul(stc[:, T1:T1 + 1], stc[:, T1:T1 + 1],
                      stc[:, NM2:NM2 + 1])
                s.add(stc[:, T0:T0 + 1], stc[:, T1:T1 + 1],
                      stc[:, T0:T0 + 1])
                C.inc(s.add(stc[:, T1:T1 + 1], stc[:, T0:T0 + 1], EPS),
                      "sS", 1)

            emit_rsqrt(blk, C, stc[:, T1:T1 + 1], stc[:, SP2:SP2 + 1],
                       stc[:, T2:T2 + 1])

            @blk.scalar
            def _(s):
                s.mul(stc[:, NM2:NM2 + 1], stc[:, NM2:NM2 + 1], -1.0)
                s.mul(stc[:, NM2:NM2 + 1], stc[:, NM2:NM2 + 1],
                      stc[:, SP2:SP2 + 1])
                s.activation(hbig[:], hbig[:], AF.Copy, bias=0.0,
                             scale=stc[:, SP2:SP2 + 1])
                C.inc(s.add(hbig[:], hbig[:], stc[:, NM2:NM2 + 1]), "sS", 1)

            @blk.vector
            def _(v, ln2g_r=ln2g_r):
                C.w(v, "sS")
                C.inc(v.scalar_tensor_tensor(hbig[:], hbig[:], 1.0, ln2g_sb[:],
                                             ALU.mult, ALU.mult), "sV", 1)

            @blk.gpsimd
            def _(g, ln2b_r=ln2b_r):
                C.w(g, "sV")
                C.inc(g.tensor_add(hbig[:], hbig[:], ln2b_sb[:]), "sG", 1)

            @blk.vector
            def _(v):
                C.w(v, "sG")
                C.inc(v.reduce_max(stc[:, AM2:AM2 + 1], hbig[:], axis=AX.X,
                                   apply_absolute_value=True), "sV", 1)

            @blk.scalar
            def _(s, stc=stc):
                C.w(s, "sV")
                s.activation(stc[:, T0:T0 + 1], stc[:, AM2:AM2 + 1], AF.Relu,
                             bias=-1e-5, scale=1.0)
                s.add(stc[:, T0:T0 + 1], stc[:, T0:T0 + 1], 1e-5)
                C.inc(s.mul(stc[:, T1:T1 + 1], stc[:, T0:T0 + 1],
                            cst["mw_cols"][:, 1:2]), "sS", 1)

            @blk.vector
            def _(v):
                C.w(v, "sS")
                v.reciprocal(stc[:, SQ:SQ + 1], stc[:, T0:T0 + 1])
                C.inc(v.reciprocal(stc[:, SQ + 1:SQ + 2],
                                   stc[:, T1:T1 + 1]), "sV", 1)

            @blk.scalar
            def _(s, i=i):
                C.w(s, "sV")
                s.mul(stc[:, SA:SA + 1], stc[:, SQ:SQ + 1], 127.0)
                s.mul(inv2_cols[:, i:i + 1], stc[:, T1:T1 + 1], 1.0 / 127.0)
                s.mul(r2_cols[:, i:i + 1], stc[:, SQ + 1:SQ + 2], 127.0)
                C.inc(s.activation(q2i8[:], hbig[:], AF.Copy, bias=0.0,
                                   scale=stc[:, SA:SA + 1]), "sS", 1)
            q2act_done[i] = C.cnt["sS"]

            if dbg:
                @blk.sync
                def _(sy, i=i):
                    C.w(sy, "sS")
                    C.inc(sy.dma_start(out=h4dbg_t[i * 128:(i + 1) * 128, :],
                                       in_=hbig[:]), "sD", 16)

            @blk.vector
            def _(v, i=i):
                C.w(v, "sS")
                if i >= 1:
                    v.wait_ge(C.sem["sD"], q2st_done[i - 1])
                C.inc(v.tensor_copy(q2bf[:], q2i8[:]), "sV", 1)

            @blk.sync
            def _(sy, i=i, q2_hbm=q2_hbm):
                C.w(sy, "sV")
                C.inc(sy.dma_start(out=q2_hbm[i * 128:(i + 1) * 128, :],
                                   in_=q2bf[:]), "sD", 16)
            q2st_done_set(i, C.cnt["sD"])

    with contextlib.ExitStack() as pes:
        psb = lambda name, shape, dt=F32: pes.enter_context(
            nc.sbuf_tensor(name + _sfx, shape, dt))
        pps = lambda name, shape: pes.enter_context(
            nc.psum_tensor(name + _sfx, shape, F32))
        qw2_sb = psb("qw2_sb", [128, 32 * D], BF16)
        b2_sb = psb("b2_sb", [128, D], BF16)
        q2fm = [psb(f"ffb_q2fm{j}", [128, 32 * 128], BF16) for j in range(2)]
        xr = [psb(f"ffb_xr{j}", [128, D]) for j in range(2)]
        ot = [psb(f"ffb_ot{j}", [128, D]) for j in range(2)]
        xo = [psb(f"ffb_xo{j}", [128, D]) for j in range(2)]
        diag2 = psb("ffb_diag2", [128, 128], BF16)
        o_ps = pps("ffb_ops", [128, 2 * D])

        @blk.sync
        def _(sy):
            C.w(sy, "sD")
            for k in range(32):
                C.inc(sy.dma_start(out=qw2_sb[:, k * D:(k + 1) * D],
                                   in_=qw2_hbm[k * 128:(k + 1) * 128, :]),
                      "sD", 16)

        @blk.gpsimd
        def _(g):
            C.inc(g.dma_start(out=b2_sb[:], in_=b2_r), "sE", 16)
        nw2 = C.cnt["sD"]
        nw2e = C.cnt["sE"]

        for i in range(n_tiles):
            g = i % 2
            bq = q2fm[g]
            bx = xr[g]
            bo = ot[g]
            bxo = xo[g]

            @blk.sync
            def _(sy, i=i, g=g, bq=bq, bx=bx):
                if i >= 2:
                    C.w(sy, "sT", mm2_done[i - 2])
                for k in range(32):
                    C.inc(sy.dma_start_transpose(
                        bq[:, k * 128:(k + 1) * 128],
                        q2_hbm[i * 128:(i + 1) * 128, k * 128:(k + 1) * 128]),
                        "sD", 16)
                C.inc(sy.dma_start(out=bx[:],
                                   in_=res_in[i * 128:(i + 1) * 128, :]),
                      "sD", 16)
            nldb = C.cnt["sD"]

            @blk.vector
            def _(v, i=i):
                if i >= 2:
                    C.w(v, "sS", evb_done[i - 2])
                C.inc(v.tensor_scalar(diag2[:], cst["ident_bf"][:],
                                      r2_cols[:, i:i + 1], None, ALU.mult),
                      "sV", 1)

            @blk.tensor
            def _(t, i=i, g=g, bq=bq):
                t.wait_ge(C.sem["sD"], nldb)
                if i == 0:
                    t.wait_ge(C.sem["sD"], nw2)
                    t.wait_ge(C.sem["sE"], nw2e)
                C.w(t, "sV")
                if i >= 2:
                    C.w(t, "sS", evb_done[i - 2])
                t.ldweights(diag2[:])
                po = o_ps[:, g * D:(g + 1) * D]
                for b in range(2):
                    t.matmul(po[:, b * 512:(b + 1) * 512], diag2[:],
                             b2_sb[:, b * 512:(b + 1) * 512],
                             start=True, stop=False)
                for k in range(32):
                    t.ldweights(bq[:, k * 128:(k + 1) * 128])
                    for b in range(2):
                        ii = t.matmul(
                            po[:, b * 512:(b + 1) * 512],
                            bq[:, k * 128:(k + 1) * 128],
                            qw2_sb[:, k * D + b * 512:k * D + (b + 1) * 512],
                            start=False, stop=(k == 31))
                C.inc(ii, "sT", 1)
            mm2_done_set(i, C.cnt["sT"])

            @blk.scalar
            def _(s, i=i, g=g, bo=bo):
                C.w(s, "sT")
                scale_col = inv2_cols[:, i:i + 1]
                C.inc(s.activation(bo[:], o_ps[:, g * D:(g + 1) * D], AF.Copy,
                                   bias=0.0, scale=scale_col), "sS", 1)
            evb_done_set(i, C.cnt["sS"])

            @blk.vector
            def _(v, i=i, bo=bo, bx=bx, bxo=bxo):
                C.w(v, "sS")
                if i >= 2:
                    v.wait_ge(C.sem["sD"], xst_done[i - 2])
                C.inc(v.scalar_tensor_tensor(bxo[:], bo[:], 0.5, bx[:],
                                             ALU.mult, ALU.add), "sV", 1)

            @blk.sync
            def _(sy, i=i, bxo=bxo):
                C.w(sy, "sV")
                if x_final is not None:
                    dst = x_final[i * 128:(i + 1) * 128, :]
                else:
                    dst = x_out[i * 128:(i + 1) * 128, :]
                C.inc(sy.dma_start(out=dst, in_=bxo[:]), "sD", 16)
            xst_done_set(i, C.cnt["sD"])


def m1_phase(nc, blk, C, cst, *, x1_hbm, pln_hbm, z_hbm, xbc_hbm, acum_hbm,
             arow_hbm, inpj, nrmw_r, dtbias_r, alog_r, convw_r, convb_r, hm_r,
             dt_tm, dtd_tm, expa_tm, eacd_tm, alast_cm,
             drep, cdrep, dk_rep):
    NGRP = 9
    with contextlib.ExitStack() as pes:
        psb = lambda name, shape, dt=F32: pes.enter_context(
            nc.sbuf_tensor(name, shape, dt))
        pps = lambda name, shape: pes.enter_context(
            nc.psum_tensor(name, shape, F32))
        inpj_bf = psb("inpj_bf", [128, 8 * DPROJ], BF16)
        nrmw_sb = psb("nrmw_sb", [128, D])
        dtb_sb = psb("dtb_sb", [128, NH])
        alog_sb = psb("alog_sb", [128, NH])
        aexp_sb = psb("aexp_sb", [128, NH])
        convw_sb = psb("convw_sb", [128, 72])
        convb_sb = psb("convb_sb", [128, 18])
        hm_sb = psb("hm_sb", [128, 1])
        dconv = psb("dconv", [128, 72 * 128], BF16)
        bx = [psb(f"m1x{j}", [128, D]) for j in range(2)]
        pb = psb("m1pln", [128, D])
        px = psb("m1px", [128, D])
        xnb = psb("m1xnb", [128, D], BF16)
        xn_fm = psb("m1xnfm", [128, 8 * 512], BF16)
        stc = psb("m1stc", [128, 16])
        dts = psb("m1dts", [128, 4 * NH])
        z_sb = psb("m1z", [128, DIN])
        xbc_sb = psb("m1xbc", [128, 18 * 520], BF16)
        cvo_bf = psb("m1cvo", [128, 512], BF16)
        av_tm = psb("m1avtm", [128, NH])
        scr32 = psb("m1scr32", [NH, 128])
        acum32 = psb("m1acum32", [NH, 128])
        stage = psb("m1stage", [1, NT_OWN * NH])
        alast_tm = psb("m1alast_tm", [NT_OWN, NH])
        excl_cm = psb("m1excl", [NH, NT_OWN])
        dst_cm = psb("m1dstcm", [NH, CHUNK])
        trsc = psb("m1trsc", [128, 128])
        ps_all = pps("m1ps", [128, 4096])
        GW = 520
        SQ, SM, NM, SP, RS2, T0, T1, T2 = range(8)

        @blk.sync
        def _(sy):
            C.inc(sy.dma_start(out=nrmw_sb[:], in_=nrmw_r), "sD", 16)
            C.inc(sy.dma_start(out=dtb_sb[:], in_=dtbias_r), "sD", 16)
            C.inc(sy.dma_start(out=alog_sb[:], in_=alog_r), "sD", 16)
            C.inc(sy.dma_start(out=convw_sb[:], in_=convw_r), "sD", 16)
            C.inc(sy.dma_start(out=convb_sb[:], in_=convb_r), "sD", 16)
            C.inc(sy.dma_start(out=hm_sb[:], in_=hm_r), "sD", 16)

        @blk.gpsimd
        def _(g):
            for k in range(8):
                C.inc(g.dma_start(out=inpj_bf[:, k * DPROJ:(k + 1) * DPROJ],
                                  in_=inpj[k * 128:(k + 1) * 128, :]),
                      "sE", 16)
        nload = C.cnt["sD"]
        nload_e = C.cnt["sE"]

        @blk.vector
        def _(v):
            v.wait_ge(C.sem["sD"], nload)
            v.wait_ge(C.sem["sE"], nload_e)
            for gj in range(72):
                ii = v.tensor_scalar(dconv[:, gj * 128:(gj + 1) * 128],
                                     cst["ident_bf"][:],
                                     convw_sb[:, gj:gj + 1], None, ALU.mult)
            C.inc(ii, "sV", 1)

        @blk.scalar
        def _(s):
            s.wait_ge(C.sem["sD"], nload)
            s.wait_ge(C.sem["sE"], nload_e)
            C.inc(s.activation(aexp_sb[:], alog_sb[:], AF.Exp), "sS", 1)

        @blk.gpsimd
        def _(g):
            for gg in range(18):
                ii = g.memset(xbc_sb[:, gg * GW:gg * GW + 3], 0.0)
            C.inc(ii, "sG", 1)

        evac_done = {}
        conv_done = {}

        for p in range(NGRP):
            tiles = list(range(4 * p, min(4 * p + 4, NT)))
            own = [i for i in tiles if i >= 1]

            for j, i in enumerate(tiles):
                b = bx[i % 2]

                @blk.sync
                def _(sy, i=i, b=b):
                    if i >= 2:
                        C.w(sy, "sS", evac_done.get(i - 2, 0))
                    C.inc(sy.dma_start(out=b[:],
                                       in_=x1_hbm[i * 128:(i + 1) * 128, :]),
                          "sD", 16)
                nld = C.cnt["sD"]

                @blk.vector
                def _(v, i=i, b=b, nld=nld):
                    v.wait_ge(C.sem["sD"], nld)
                    if i >= 1:
                        C.w(v, "sS")
                    C.inc(v.reduce_sum(stc[:, SM:SM + 1], b[:], axis=AX.X),
                          "sV", 1)

                @blk.scalar
                def _(s, b=b):
                    C.w(s, "sV")
                    C.w(s, "sT")
                    C.inc(s.activation(ps_all[:, 0:D], b[:], AF.Square,
                                       bias=0.0, scale=1.0,
                                       accum_out=stc[:, SQ:SQ + 1]), "sS", 1)

                @blk.scalar
                def _(s):
                    s.mul(stc[:, NM:NM + 1], stc[:, SM:SM + 1], 1.0 / D)
                    s.mul(stc[:, T0:T0 + 1], stc[:, SQ:SQ + 1], 1.0 / D)
                    s.mul(stc[:, T1:T1 + 1], stc[:, NM:NM + 1], -1.0)
                    s.mul(stc[:, T1:T1 + 1], stc[:, T1:T1 + 1],
                          stc[:, NM:NM + 1])
                    s.add(stc[:, T0:T0 + 1], stc[:, T1:T1 + 1],
                          stc[:, T0:T0 + 1])
                    C.inc(s.add(stc[:, T1:T1 + 1], stc[:, T0:T0 + 1], EPS),
                          "sS", 1)

                emit_rsqrt(blk, C, stc[:, T1:T1 + 1], stc[:, SP:SP + 1],
                           stc[:, T2:T2 + 1])

                @blk.scalar
                def _(s):
                    s.mul(stc[:, T1:T1 + 1], stc[:, T0:T0 + 1],
                          stc[:, SP:SP + 1])
                    s.mul(stc[:, T1:T1 + 1], stc[:, T1:T1 + 1],
                          stc[:, SP:SP + 1])
                    C.inc(s.add(stc[:, T1:T1 + 1], stc[:, T1:T1 + 1], EPS),
                          "sS", 1)

                emit_rsqrt(blk, C, stc[:, T1:T1 + 1], stc[:, RS2:RS2 + 1],
                           stc[:, T2:T2 + 1])

                @blk.scalar
                def _(s, b=b, i=i):
                    s.mul(stc[:, NM:NM + 1], stc[:, NM:NM + 1], -1.0)
                    s.mul(stc[:, NM:NM + 1], stc[:, NM:NM + 1],
                          stc[:, SP:SP + 1])
                    s.activation(pb[:], b[:], AF.Copy, bias=0.0,
                                 scale=stc[:, SP:SP + 1])
                    s.add(pb[:], pb[:], stc[:, NM:NM + 1])
                    C.inc(s.mul(px[:], pb[:], stc[:, RS2:RS2 + 1]), "sS", 1)
                evac_done[i] = C.cnt["sS"]

                @blk.sync
                def _(sy, i=i):
                    C.w(sy, "sS")
                    C.inc(sy.dma_start(out=pln_hbm[i * 128:(i + 1) * 128, :],
                                       in_=pb[:]), "sD", 16)

                @blk.vector
                def _(v):
                    C.w(v, "sS")
                    C.inc(v.scalar_tensor_tensor(xnb[:], px[:], 1.0,
                                                 nrmw_sb[:], ALU.mult,
                                                 ALU.mult), "sV", 1)

                @blk.sync
                def _(sy, p=p, j=j):
                    C.w(sy, "sV")
                    if p > 0 or j > 0:
                        C.w(sy, "sT")
                    for k in range(8):
                        C.inc(sy.dma_start_transpose(
                            xn_fm[:, k * 512 + j * 128:
                                  k * 512 + (j + 1) * 128],
                            xnb[:, k * 128:(k + 1) * 128]), "sD", 16)
                ntr = C.cnt["sD"]

                if i >= 1 and M1STAGE >= 2:
                    @blk.tensor
                    def _(t, j=j, ntr=ntr):
                        t.wait_ge(C.sem["sD"], ntr)
                        C.w(t, "sS")
                        C.w(t, "sV")
                        base = j * 128
                        for k in range(8):
                            lw = xn_fm[:, k * 512 + base:k * 512 + base + 128]
                            t.ldweights(lw)
                            for bnk in range(4):
                                t.matmul(
                                    ps_all[:, bnk * 512:(bnk + 1) * 512],
                                    lw,
                                    inpj_bf[:, k * DPROJ + bnk * 512:
                                            k * DPROJ + bnk * 512 + 512],
                                    start=(k == 0), stop=(k == 7))
                            ii = t.matmul(ps_all[:, 2048:2080], lw,
                                          inpj_bf[:, k * DPROJ + DIN + CONVD:
                                                  k * DPROJ + DPROJ],
                                          start=(k == 0), stop=(k == 7))
                        C.inc(ii, "sT", 1)

                    @blk.vector
                    def _(v, i=i):
                        C.w(v, "sT")
                        if i >= 2:
                            C.w(v, "sD")
                        v.tensor_copy(z_sb[:], ps_all[:, 0:DIN])
                        C.inc(v.scalar_tensor_tensor(
                            dts[:, 0:NH], ps_all[:, 2048:2080], 1.0,
                            dtb_sb[:], ALU.mult, ALU.add), "sV", 1)

                    @blk.sync
                    def _(sy, i=i):
                        C.w(sy, "sV")
                        C.inc(sy.dma_start(
                            out=z_hbm[(i - 1) * 128:i * 128, :],
                            in_=z_sb[:]), "sD", 16)

                    @blk.scalar
                    def _(s):
                        C.w(s, "sV")
                        s.activation(dts[:, NH:2 * NH], dts[:, 0:NH], AF.Abs)
                        s.activation(dts[:, NH:2 * NH], dts[:, NH:2 * NH],
                                     AF.Exp, bias=0.0, scale=-1.0)
                        s.activation(dts[:, NH:2 * NH], dts[:, NH:2 * NH],
                                     AF.Ln, bias=1.0, scale=1.0)
                        C.inc(s.activation(dts[:, 2 * NH:3 * NH],
                                           dts[:, 0:NH], AF.Relu), "sS", 1)

                    @blk.gpsimd
                    def _(g, i=i):
                        C.w(g, "sS")
                        C.inc(g.tensor_add(
                            dt_tm[:, (i - 1) * NH:i * NH],
                            dts[:, 2 * NH:3 * NH], dts[:, NH:2 * NH]),
                            "sG", 1)

                    @blk.vector
                    def _(v, i=i):
                        C.w(v, "sG")
                        C.inc(v.scalar_tensor_tensor(
                            av_tm[:, 0:NH],
                            dt_tm[:, (i - 1) * NH:i * NH], -1.0,
                            aexp_sb[:], ALU.mult, ALU.mult), "sV", 1)

                    c = i - 1

                    @blk.tensor
                    def _(t):
                        C.w(t, "sV")
                        C.inc(t.transpose(ps_all[0:NH, 2304:2432],
                                          av_tm[:, 0:NH],
                                          cst["ident"][:]), "sT", 1)

                    @blk.scalar
                    def _(s):
                        C.w(s, "sT")
                        C.inc(s.copy(scr32[:, :], ps_all[0:NH, 2304:2432]),
                              "sS", 1)

                    @blk.vector
                    def _(v):
                        C.w(v, "sS")
                        C.inc(v.tensor_tensor_scan(
                            acum32[:, :], scr32[:, :], scr32[:, :],
                            0.0, ALU.add, ALU.bypass), "sV", 1)

                    @blk.scalar
                    def _(s, c=c):
                        C.w(s, "sV")
                        s.copy(alast_cm[:, c:c + 1], acum32[:, 127:128])
                        s.copy(scr32[:, 0:1], acum32[:, 0:1])
                        s.copy(scr32[:, 1:2], acum32[:, 1:2])
                        C.inc(s.activation(dst_cm[:, 0:128], acum32[:, :],
                                           AF.Exp,
                                           bias=alast_cm[:, c:c + 1],
                                           scale=-1.0), "sS", 1)

                    @blk.sync
                    def _(sy, c=c):
                        C.w(sy, "sV")
                        C.inc(sy.dma_start(
                            out=acum_hbm[:, c * 128:(c + 1) * 128],
                            in_=acum32[:, :]), "sD", 16)

                    @blk.tensor
                    def _(t):
                        C.w(t, "sS")
                        t.transpose(ps_all[0:128, 2304:2304 + NH],
                                    dst_cm[:, 0:128],
                                    cst["ident"][0:NH, 0:NH])
                        C.inc(t.transpose(ps_all[0:128, 2368:2368 + NH],
                                          acum32[:, :],
                                          cst["ident"][0:NH, 0:NH]),
                              "sT", 1)

                    @blk.scalar
                    def _(s, c=c):
                        C.w(s, "sT")
                        s.copy(dts[:, 3 * NH:4 * NH],
                               ps_all[0:128, 2304:2304 + NH])
                        C.inc(s.copy(expa_tm[:, c * NH:(c + 1) * NH],
                                     ps_all[0:128, 2368:2368 + NH]), "sS", 1)

                    @blk.gpsimd
                    def _(g, c=c):
                        C.w(g, "sS")
                        C.inc(g.tensor_mul(dtd_tm[:, c * NH:(c + 1) * NH],
                                           dt_tm[:, c * NH:(c + 1) * NH],
                                           dts[:, 3 * NH:4 * NH]), "sG", 1)

            if M1STAGE < 3:
                continue
            gl = len(tiles) * 128
            ev_cnt = {}
            for gg in range(18):
                pcol = 2560 if gg % 2 == 0 else 3072

                @blk.tensor
                def _(t, p=p, gg=gg, pcol=pcol, gl=gl):
                    if gg == 0:
                        C.w(t, "sD")
                        C.w(t, "sV")
                        if p > 0:
                            C.w(t, "sT", conv_done.get(p - 1, 0))
                    if gg >= 2:
                        sem, cntv = ev_cnt[gg - 2]
                        t.wait_ge(C.sem[sem], cntv)
                    for k in range(8):
                        lw = inpj_bf[:, k * DPROJ + DIN + gg * 128:
                                     k * DPROJ + DIN + (gg + 1) * 128]
                        t.ldweights(lw)
                        ii = t.matmul(ps_all[:, pcol:pcol + gl], lw,
                                      xn_fm[:, k * 512:k * 512 + gl],
                                      start=(k == 0), stop=(k == 7))
                    C.inc(ii, "sT", 1)
                need = C.cnt["sT"]

                if gg % 2 == 0:
                    @blk.vector
                    def _(v, gg=gg, pcol=pcol, need=need, gl=gl):
                        v.wait_ge(C.sem["sT"], need)
                        C.w(v, "sS")
                        C.inc(v.tensor_copy(
                            xbc_sb[:, gg * GW + 3:gg * GW + 3 + gl],
                            ps_all[:, pcol:pcol + gl]), "sV", 1)
                    ev_cnt[gg] = ("sV", C.cnt["sV"])
                else:
                    @blk.scalar
                    def _(s, gg=gg, pcol=pcol, need=need, gl=gl):
                        s.wait_ge(C.sem["sT"], need)
                        C.inc(s.copy(
                            xbc_sb[:, gg * GW + 3:gg * GW + 3 + gl],
                            ps_all[:, pcol:pcol + gl]), "sS", 1)
                    ev_cnt[gg] = ("sS", C.cnt["sS"])

            if p == 0:
                @blk.scalar
                def _(s):
                    C.w(s, "sV")
                    for gg in range(18):
                        ii = s.mul(xbc_sb[:, gg * GW + 3:gg * GW + 131],
                                   xbc_sb[:, gg * GW + 3:gg * GW + 131],
                                   hm_sb[:])
                    C.inc(ii, "sS", 1)

            if M1STAGE < 4:
                continue
            own_lo = 128 if p == 0 else 0
            out_lo = 0 if p == 0 else 4 * p * 128 - HALO
            own_n = gl - own_lo
            for gg in range(18):
                ccol = 3584 if gg % 2 == 0 else 0
                @blk.tensor
                def _(t, gg=gg, ccol=ccol, gl=gl):
                    C.w(t, "sV")
                    C.w(t, "sS")
                    if gg < 2:
                        C.w(t, "sG")
                    for jj in range(4):
                        t.ldweights(dconv[:, (gg * 4 + jj) * 128:
                                          (gg * 4 + jj + 1) * 128])
                        ii = t.matmul(ps_all[:, ccol:ccol + gl],
                                      dconv[:, (gg * 4 + jj) * 128:
                                            (gg * 4 + jj + 1) * 128],
                                      xbc_sb[:, gg * GW + jj:gg * GW + jj + gl],
                                      start=(jj == 0), stop=(jj == 3))
                    C.inc(ii, "sT", 1)

                @blk.scalar
                def _(s, gg=gg, ccol=ccol, own_lo=own_lo, own_n=own_n):
                    C.w(s, "sT")
                    if gg >= 1:
                        C.w(s, "sD")
                    C.inc(s.activation(cvo_bf[:, 0:own_n],
                                       ps_all[:, ccol + own_lo:
                                              ccol + own_lo + own_n],
                                       AF.Silu, bias=convb_sb[:, gg:gg + 1],
                                       scale=1.0), "sS", 1)

                @blk.sync
                def _(sy, gg=gg, out_lo=out_lo, own_n=own_n):
                    C.w(sy, "sS")
                    C.inc(sy.dma_start(
                        out=xbc_hbm[gg * 128:(gg + 1) * 128,
                                    out_lo:out_lo + own_n],
                        in_=cvo_bf[:, 0:own_n]), "sD", 16)

            if p < NGRP - 1:
                @blk.scalar
                def _(s, gl=gl):
                    for gg in range(18):
                        ii = s.copy(xbc_sb[:, gg * GW:gg * GW + 3],
                                    xbc_sb[:, gg * GW + gl:gg * GW + gl + 3])
                    C.inc(ii, "sS", 1)
            conv_done[p] = C.cnt["sT"]

        if M1STAGE < 9:
            return
        @blk.scalar
        def _(s):
            C.w(s, "sG")
            C.inc(s.activation(expa_tm[:], expa_tm[:], AF.Exp), "sS", 1)

        @blk.vector
        def _(v):
            C.w(v, "sS")
            C.inc(v.tensor_tensor_scan(excl_cm[0:NH, 0:NT_OWN],
                                       alast_cm[0:NH, 0:NT_OWN],
                                       alast_cm[0:NH, 0:NT_OWN],
                                       0.0, ALU.add, ALU.bypass), "sV", 1)

        @blk.scalar
        def _(s):
            C.w(s, "sV")
            s.copy(trsc[0:NH, 0:NT_OWN], excl_cm[0:NH, 0:NT_OWN])
            s.memzero(excl_cm[0:NH, 0:1])
            C.inc(s.copy(excl_cm[0:NH, 1:NT_OWN],
                         trsc[0:NH, 0:NT_OWN - 1]), "sS", 1)

        @blk.tensor
        def _(t):
            C.w(t, "sS")
            t.transpose(ps_all[0:NT_OWN, 0:NH], alast_cm[0:NH, 0:NT_OWN],
                        cst["ident"][0:NH, 0:NH])
            C.inc(t.transpose(ps_all[0:NT_OWN, 512:512 + NH],
                              excl_cm[0:NH, 0:NT_OWN],
                              cst["ident"][0:NH, 0:NH]), "sT", 1)

        @blk.scalar
        def _(s):
            C.w(s, "sT")
            s.copy(alast_tm[:, :], ps_all[0:NT_OWN, 0:NH])
            C.inc(s.copy(trsc[0:NT_OWN, 0:NH],
                         ps_all[0:NT_OWN, 512:512 + NH]), "sS", 1)

        @blk.sync
        def _(sy):
            C.w(sy, "sS")
            C.inc(sy.dma_start(out=arow_hbm[0:1, :], in_=alast_tm[:, :]),
                  "sD", 16)
            C.inc(sy.dma_start(out=arow_hbm[1:2, :], in_=trsc[0:NT_OWN, 0:NH]),
                  "sD", 16)
        narow = C.cnt["sD"]

        @blk.sync
        def _(sy):
            sy.wait_ge(C.sem["sD"], narow)
            C.inc(sy.dma_start(out=stage[0:1, :], in_=arow_hbm[0:1, :]),
                  "sD", 16)
        nst1 = C.cnt["sD"]

        @blk.scalar
        def _(s):
            s.wait_ge(C.sem["sD"], nst1)
            C.inc(s.activation(stage[0:1, :], stage[0:1, :], AF.Exp), "sS", 1)

        @blk.tensor
        def _(t):
            C.w(t, "sS")
            t.matmul(ps_all[:, 0:512], cst["ones1"][:], stage[0:1, 0:512],
                     start=True, stop=True)
            C.inc(t.matmul(ps_all[:, 512:1024], cst["ones1"][:],
                           stage[0:1, 512:1024], start=True, stop=True),
                  "sT", 1)

        @blk.scalar
        def _(s):
            C.w(s, "sT")
            C.inc(s.copy(drep[:, :], ps_all[:, 0:1024]), "sS", 1)

        @blk.sync
        def _(sy):
            C.w(sy, "sS")
            C.inc(sy.dma_start(out=stage[0:1, :], in_=arow_hbm[1:2, :]),
                  "sD", 16)
        nst2 = C.cnt["sD"]

        @blk.scalar
        def _(s):
            s.wait_ge(C.sem["sD"], nst2)
            C.inc(s.activation(stage[0:1, :], stage[0:1, :], AF.Exp), "sS", 1)

        @blk.tensor
        def _(t):
            C.w(t, "sS")
            t.matmul(ps_all[:, 0:512], cst["ones1"][:], stage[0:1, 0:512],
                     start=True, stop=True)
            C.inc(t.matmul(ps_all[:, 512:1024], cst["ones1"][:],
                           stage[0:1, 512:1024], start=True, stop=True),
                  "sT", 1)

        @blk.scalar
        def _(s):
            C.w(s, "sT")
            C.inc(s.copy(cdrep[:, :], ps_all[:, 0:1024]), "sS", 1)

        @blk.scalar
        def _(s):
            s.add(trsc[0:NH, 0:1], excl_cm[0:NH, NT_OWN - 1:NT_OWN],
                  alast_cm[0:NH, NT_OWN - 1:NT_OWN])
            C.inc(s.copy(trsc[0:NH, 1:2], trsc[0:NH, 0:1]), "sS", 1)

        @blk.tensor
        def _(t):
            C.w(t, "sS")
            C.inc(t.transpose(ps_all[0:1, 0:NH], trsc[0:NH, 0:1],
                              cst["ident"][0:NH, 0:NH]), "sT", 1)

        @blk.scalar
        def _(s):
            C.w(s, "sT")
            C.inc(s.activation(stage[0:1, 0:NH], ps_all[0:1, 0:NH], AF.Exp),
                  "sS", 1)

        @blk.tensor
        def _(t):
            C.w(t, "sS")
            C.inc(t.matmul(ps_all[:, 0:NH], cst["ones1"][:],
                           stage[0:1, 0:NH], start=True, stop=True), "sT", 1)

        @blk.scalar
        def _(s):
            C.w(s, "sT")
            C.inc(s.copy(dk_rep[:, :], ps_all[:, 0:NH]), "sS", 1)

        for c in range(NT_OWN):
            @blk.gpsimd
            def _(g, c=c):
                C.w(g, "sS")
                C.inc(g.tensor_mul(eacd_tm[:, c * NH:(c + 1) * NH],
                                   expa_tm[:, c * NH:(c + 1) * NH],
                                   cdrep[:, c * NH:(c + 1) * NH]), "sG", 1)




def m2_phase(nc, blk, C, cst, *, xbc_hbm, acum_hbm, y_hbm, dt_tm, dtd_tm,
             expa_tm, drep, dskip_r, p_state, cc_in, cc_out, dk_rep,
             mask_r, sel_r, r_state, m2dbg_t=None, rdbg2_t=None):
    with contextlib.ExitStack() as pes:
        psb = lambda name, shape, dt=F32: pes.enter_context(
            nc.sbuf_tensor(name, shape, dt))
        pps = lambda name, shape: pes.enter_context(
            nc.psum_tensor(name, shape, F32))
        stg = psb("m2stg", [1, NH * 128])
        nstg = psb("m2nstg", [1, NH * 128])
        xs_tm = psb("m2xs", [128, DIN], BF16)
        xsdt = psb("m2xsdt", [128, DIN], BF16)
        xsdtd = psb("m2xsdtd", [128, DIN], BF16)
        b_tm = psb("m2btm", [128, 128], BF16)
        b_cm = psb("m2bcm", [128, 128], BF16)
        c_cm = psb("m2ccm", [128, 128], BF16)
        scT = psb("m2sct", [128, 128], BF16)
        dm_sb = psb("m2dm", [128, 1024])
        expdm = psb("m2expdm", [128, 1024], BF16)
        mhat = psb("m2mhat", [128, 1024], BF16)
        yde = psb("m2yde", [128, DIN])
        yofft = psb("m2yofft", [128, DIN])
        ste = psb("m2ste", [128, DIN])
        skp = psb("m2skp", [128, DIN])
        yout = psb("m2yout", [128, DIN])
        ptmp = psb("m2ptmp", [128, DIN])
        p_bf = psb("m2pbf", [128, DIN], BF16)
        dskip_sb = psb("m2dskip", [128, NH])
        mask_sb = psb("m2mask", [128, 8])
        sel_sb = psb("m2sel", [128, 64])
        agS = psb("m2agS", [128, DIN])
        agD = psb("m2agD", [128, NH])
        dm1 = psb("m2dm1", [128, 8 * NH])
        wj = psb("m2wj", [128, NH])
        wtmp = psb("m2wtmp", [128, NH])
        m2ps = pps("m2ps", [128, 4096])

        @blk.sync
        def _(sy):
            C.inc(sy.dma_start(out=dskip_sb[:], in_=dskip_r), "sD", 16)
            C.inc(sy.dma_start(out=mask_sb[:], in_=mask_r), "sD", 16)
            C.inc(sy.dma_start(out=sel_sb[:], in_=sel_r), "sD", 16)
        nl0 = C.cnt["sD"]

        @blk.gpsimd
        def _(g):
            g.memset(p_state[:], 0.0)
            g.memset(p_bf[:], 0.0)
            C.inc(g.nop(), "sG", 1)
        npz = C.cnt["sG"]

        for c in range(NT_OWN):
            sl = slice(c * 128, (c + 1) * 128)

            @blk.sync
            def _(sy, c=c, sl=sl):
                if c == 0:
                    sy.wait_ge(C.sem["sD"], nl0)
                else:
                    C.w(sy, "sT")
                    C.w(sy, "sG")
                C.inc(sy.dma_start(out=stg[0:1, :], in_=acum_hbm[:, sl]),
                      "sD", 16)
                C.inc(sy.dma_start_transpose(xs_tm[:], xbc_hbm[0:DIN, sl]),
                      "sD", 16)
                C.inc(sy.dma_start_transpose(
                    b_tm[:], xbc_hbm[DIN:DIN + 128, sl]), "sD", 16)
                C.inc(sy.dma_start(out=b_cm[:],
                                   in_=xbc_hbm[DIN:DIN + 128, sl]), "sD", 16)
                C.inc(sy.dma_start(out=c_cm[:],
                                   in_=xbc_hbm[DIN + 128:DIN + 256, sl]),
                      "sD", 16)
            nld = C.cnt["sD"]

            @blk.scalar
            def _(s, nld=nld):
                s.wait_ge(C.sem["sD"], nld)
                C.inc(s.activation(nstg[0:1, :], stg[0:1, :], AF.Copy,
                                   bias=0.0, scale=-1.0), "sS", 1)
            nneg = C.cnt["sS"]

            @blk.vector
            def _(v, c=c, nld=nld):
                v.wait_ge(C.sem["sD"], nld)
                if c >= 1:
                    C.w(v, "sT")
                dtp = dt_tm[:, c * NH:(c + 1) * NH].unsqueeze(2) \
                    .broadcast_to([128, NH, HD])
                dtdp = dtd_tm[:, c * NH:(c + 1) * NH].unsqueeze(2) \
                    .broadcast_to([128, NH, HD])
                xv = xs_tm[:].rearrange("p (h e) -> p h e", e=HD)
                v.tensor_tensor(xsdt[:].rearrange("p (h e) -> p h e", e=HD),
                                xv, dtp, ALU.mult)
                C.inc(v.tensor_tensor(
                    xsdtd[:].rearrange("p (h e) -> p h e", e=HD),
                    xv, dtdp, ALU.mult), "sV", 1)
            nxs = C.cnt["sV"]

            @blk.tensor
            def _(t, c=c, nld=nld):
                t.wait_ge(C.sem["sD"], nld)
                if c >= 1:
                    C.w(t, "sV")
                    C.w(t, "sS")
                t.ldweights(b_cm[:])
                C.inc(t.matmul(m2ps[:, 0:128], b_cm[:], c_cm[:],
                               start=True, stop=True), "sT", 1)
            nsc = C.cnt["sT"]

            @blk.scalar
            def _(s, nsc=nsc):
                s.wait_ge(C.sem["sT"], nsc)
                C.inc(s.copy(scT[:], m2ps[:, 0:128]), "sS", 1)
            nsct = C.cnt["sS"]

            ydl = {}
            for half in range(2):
                for pair in range(2):
                    q0 = half * 4 + pair * 2

                    @blk.tensor
                    def _(t, c=c, q0=q0, half=half, pair=pair):
                        if q0 == 0:
                            t.wait_ge(C.sem["sS"], nsct)
                            C.w(t, "sV")
                            C.w(t, "sG")
                        else:
                            C.w(t, "sV")
                        for qi in range(2):
                            quad = q0 + qi
                            dmc = qi * 512
                            t.matmul(m2ps[:, dmc:dmc + 512], cst["ones1"][:],
                                     stg[0:1, quad * 512:(quad + 1) * 512],
                                     start=True, stop=False)
                            for hh in range(3):
                                t.matmul(
                                    m2ps[:, dmc + hh * 128:dmc + (hh + 1) * 128],
                                    nstg[0:1, (quad * 4 + hh) * 128:
                                         (quad * 4 + hh + 1) * 128],
                                    cst["ones1"][:],
                                    start=False, stop=False)
                            ii = t.matmul(
                                m2ps[:, dmc + 384:dmc + 512],
                                nstg[0:1, (quad * 4 + 3) * 128:
                                     (quad * 4 + 4) * 128],
                                cst["ones1"][:], start=False, stop=True)
                        C.inc(ii, "sT", 1)
                    ndm = C.cnt["sT"]

                    @blk.vector
                    def _(v, ndm=ndm):
                        v.wait_ge(C.sem["sT"], ndm)
                        C.w(v, "sS")
                        trip = cst["trineg"][:].unsqueeze(1) \
                            .broadcast_to([128, 8, 128])
                        C.inc(v.scalar_tensor_tensor(
                            dm_sb[:].rearrange("p (q e) -> p q e", e=128),
                            m2ps[:, 0:1024]
                            .rearrange("p (q e) -> p q e", e=128),
                            0.0, trip, ALU.min, ALU.add), "sV", 1)
                    ndms = C.cnt["sV"]

                    @blk.scalar
                    def _(s, ndms=ndms):
                        s.wait_ge(C.sem["sV"], ndms)
                        C.inc(s.activation(expdm[:], dm_sb[:], AF.Exp),
                              "sS", 1)
                    nexp = C.cnt["sS"]

                    @blk.vector
                    def _(v, nexp=nexp):
                        v.wait_ge(C.sem["sS"], nexp)
                        scp = scT[:].unsqueeze(1).broadcast_to([128, 8, 128])
                        C.inc(v.tensor_tensor(
                            mhat[:].rearrange("p (q e) -> p q e", e=128),
                            expdm[:].rearrange("p (q e) -> p q e", e=128),
                            scp, ALU.mult), "sV", 1)
                    nmh = C.cnt["sV"]

                    @blk.tensor
                    def _(t, c=c, q0=q0, pair=pair, half=half, nmh=nmh):
                        t.wait_ge(C.sem["sV"], nmh)
                        if q0 == 0 and c >= 1:
                            C.w(t, "sG")
                        for qi in range(2):
                            quad = q0 + qi
                            q = quad - half * 4
                            ydc = 1024 + q * 256
                            stc_ = 2048 + q * 256
                            yoc = 3072 + q * 256
                            mb = qi * 512
                            for hh in range(4):
                                h = quad * 4 + hh
                                t.ldweights(mhat[:, mb + hh * 128:
                                                 mb + (hh + 1) * 128])
                                t.matmul(m2ps[:, ydc + hh * 64:
                                              ydc + (hh + 1) * 64],
                                         mhat[:, mb + hh * 128:
                                              mb + (hh + 1) * 128],
                                         xsdt[:, h * HD:(h + 1) * HD],
                                         start=True, stop=True)
                            t.ldweights(b_tm[:])
                            for hh in range(4):
                                h = quad * 4 + hh
                                t.matmul(m2ps[:, stc_ + hh * 64:
                                              stc_ + (hh + 1) * 64],
                                         b_tm[:],
                                         xsdtd[:, h * HD:(h + 1) * HD],
                                         start=True, stop=True)
                            t.ldweights(c_cm[:])
                            for hh in range(4):
                                h = quad * 4 + hh
                                ii = t.matmul(
                                    m2ps[:, yoc + hh * 64:yoc + (hh + 1) * 64],
                                    c_cm[:], p_bf[:, h * HD:(h + 1) * HD],
                                    start=True, stop=True)
                        C.inc(ii, "sT", 1)
                ydl[half] = C.cnt["sT"]

                hb = half * 1024
                hh16 = half * 16

                @blk.scalar
                def _(s, half=half, hb=hb, need=ydl[half]):
                    s.wait_ge(C.sem["sT"], need)
                    s.copy(yde[:, hb:hb + 1024], m2ps[:, 1024:2048])
                    C.inc(s.copy(ste[:, hb:hb + 1024], m2ps[:, 2048:3072]),
                          "sS", 1)
                nev = C.cnt["sS"]

                @blk.vector
                def _(v, c=c, hb=hb, hh16=hh16, need=ydl[half]):
                    v.wait_ge(C.sem["sT"], need)
                    eap = expa_tm[:, c * NH + hh16:c * NH + hh16 + 16] \
                        .unsqueeze(2).broadcast_to([128, 16, HD])
                    C.inc(v.tensor_tensor(
                        yofft[:, hb:hb + 1024]
                        .rearrange("p (h e) -> p h e", e=HD),
                        m2ps[:, 3072:4096]
                        .rearrange("p (h e) -> p h e", e=HD),
                        eap, ALU.mult), "sV", 1)
                nyt = C.cnt["sV"]

                @blk.gpsimd
                def _(g, c=c, hb=hb, hh16=hh16, nev=nev, nyt=nyt):
                    g.wait_ge(C.sem["sS"], nev)
                    g.wait_ge(C.sem["sV"], nyt)
                    dsp = dskip_sb[:, hh16:hh16 + 16].unsqueeze(2) \
                        .broadcast_to([128, 16, HD])
                    g.tensor_tensor(
                        skp[:, hb:hb + 1024]
                        .rearrange("p (h e) -> p h e", e=HD),
                        xs_tm[:, hb:hb + 1024]
                        .rearrange("p (h e) -> p h e", e=HD),
                        dsp, ALU.mult)
                    g.tensor_add(yout[:, hb:hb + 1024], yofft[:, hb:hb + 1024],
                                 yde[:, hb:hb + 1024])
                    g.tensor_add(yout[:, hb:hb + 1024], yout[:, hb:hb + 1024],
                                 skp[:, hb:hb + 1024])
                    drp = drep[:, c * NH + hh16:c * NH + hh16 + 16] \
                        .unsqueeze(2).broadcast_to([128, 16, HD])
                    g.tensor_tensor(
                        ptmp[:, hb:hb + 1024]
                        .rearrange("p (h e) -> p h e", e=HD),
                        p_state[:, hb:hb + 1024]
                        .rearrange("p (h e) -> p h e", e=HD),
                        drp, ALU.mult)
                    g.tensor_add(p_state[:, hb:hb + 1024],
                                 ptmp[:, hb:hb + 1024],
                                 ste[:, hb:hb + 1024])
                    C.inc(g.tensor_copy(p_bf[:, hb:hb + 1024],
                                        p_state[:, hb:hb + 1024]), "sG", 1)

            @blk.sync
            def _(sy, c=c):
                C.w(sy, "sG")
                C.inc(sy.dma_start(out=y_hbm[c * 128:(c + 1) * 128, :],
                                   in_=yout[:]), "sD", 16)

        @blk.sync
        def _(sy):
            C.w(sy, "sG")
            C.inc(sy.dma_start(out=cc_in[:, 0:DIN], in_=p_state[:]),
                  "sD", 16)
            C.inc(sy.dma_start(out=cc_in[:, DIN:DIN + NH], in_=dk_rep[:]),
                  "sD", 16)
        nag = C.cnt["sD"]

        @blk.gpsimd
        def _(g):
            g.wait_ge(C.sem["sD"], nag)
            C.inc(g.collective_compute(
                "AllGather", ALU.bypass,
                replica_groups=[list(range(NCORES))],
                ins=[cc_in.opt()], outs=[cc_out.opt()]), "sC", 1)
        ncc = C.cnt["sC"]

        @blk.sync
        def _(sy):
            C.w(sy, "sC")
            for j in range(NCORES):
                C.inc(sy.dma_start(
                    out=dm1[:, j * NH:(j + 1) * NH],
                    in_=cc_out[j * 128:(j + 1) * 128, DIN:DIN + NH]),
                    "sD", 16)
        ndall = C.cnt["sD"]

        @blk.gpsimd
        def _(g):
            g.wait_ge(C.sem["sD"], ndall)
            g.memset(r_state[:], 0.0)
            C.inc(g.tensor_scalar_add(dm1[:, :], dm1[:, :], -1.0), "sG", 1)

        for j in range(NCORES):
            @blk.sync
            def _(sy, j=j):
                C.w(sy, "sG")
                C.inc(sy.dma_start(
                    out=agS[:], in_=cc_out[j * 128:(j + 1) * 128, 0:DIN]),
                    "sD", 16)
            nagj = C.cnt["sD"]

            @blk.gpsimd
            def _(g, j=j, nagj=nagj):
                g.wait_ge(C.sem["sD"], nagj)
                g.memset(wj[:], 1.0)
                g.tensor_scalar(wj[:], wj[:], mask_sb[:, j:j + 1], None,
                                ALU.mult)
                for m in range(NCORES):
                    g.tensor_scalar(wtmp[:], dm1[:, m * NH:(m + 1) * NH],
                                    sel_sb[:, j * 8 + m:j * 8 + m + 1], None,
                                    ALU.mult)
                    g.tensor_scalar_add(wtmp[:], wtmp[:], 1.0)
                    g.tensor_mul(wj[:], wj[:], wtmp[:])
                wjp = wj[:].unsqueeze(2).broadcast_to([128, NH, HD])
                g.tensor_tensor(
                    ptmp[:].rearrange("p (h e) -> p h e", e=HD),
                    agS[:].rearrange("p (h e) -> p h e", e=HD),
                    wjp, ALU.mult)
                C.inc(g.tensor_add(r_state[:], r_state[:], ptmp[:]), "sG", 1)

        if m2dbg_t is not None:
            @blk.vector
            def _(v):
                C.w(v, "sG")
                v.tensor_copy(yde[:, 0:8], mask_sb[:])
                v.tensor_copy(yde[:, 8:72], sel_sb[:])
                v.tensor_copy(yde[:, 72:328], dm1[:])
                C.inc(v.tensor_copy(yde[:, 328:360], wj[:]), "sV", 1)

            @blk.sync
            def _(sy):
                C.w(sy, "sG")
                C.inc(sy.dma_start(out=rdbg2_t, in_=r_state[:]), "sD", 16)

            @blk.sync
            def _(sy):
                C.w(sy, "sV")
                C.inc(sy.dma_start(out=m2dbg_t[:, 0:360],
                                   in_=yde[:, 0:360]), "sD", 16)


def m3_phase(nc, blk, C, cst, *, y_hbm, z_hbm, x1_hbm, pln_hbm, x2_hbm,
             xbc_hbm, outpj, gatew_r, eacd_tm, expa_tm, cdrep, r_state,
             dbg=False, ydbg_t=None, yndbg_t=None, rdbg_t=None):
    with contextlib.ExitStack() as pes:
        psb = lambda name, shape, dt=F32: pes.enter_context(
            nc.sbuf_tensor(name, shape, dt))
        pps = lambda name, shape: pes.enter_context(
            nc.psum_tensor(name, shape, F32))
        opj_bf = psb("m3opj", [128, 16 * D], BF16)
        gatew_sb = psb("m3gw", [128, DIN])
        r_bf = psb("m3rbf", [128, DIN], BF16)
        yl = psb("m3yl", [128, DIN])
        zz = psb("m3z", [128, DIN])
        x1t = psb("m3x1", [128, D])
        plt = psb("m3pln", [128, D])
        ccm2 = psb("m3ccm", [128, 128], BF16)
        ycor = psb("m3ycor", [128, DIN])
        y2 = psb("m3y2", [128, DIN])
        sz = psb("m3sz", [128, DIN])
        yg = psb("m3yg", [128, DIN])
        tb = psb("m3tb", [128, DIN])
        ynb = psb("m3ynb", [128, DIN], BF16)
        yn_fm = psb("m3ynfm", [128, 16 * 128], BF16)
        x2a = psb("m3x2a", [128, D])
        x2t = psb("m3x2t", [128, D])
        stc = psb("m3stc", [128, 8])
        m3ps = pps("m3ps", [128, 4096])
        SQc, RS, T0, T1 = range(4)

        @blk.sync
        def _(sy):
            C.inc(sy.dma_start(out=gatew_sb[:], in_=gatew_r), "sD", 16)

        @blk.gpsimd
        def _(g):
            for k in range(16):
                C.inc(g.dma_start(out=opj_bf[:, k * D:(k + 1) * D],
                                  in_=outpj[k * 128:(k + 1) * 128, :]),
                      "sE", 16)
            C.inc(g.tensor_copy(r_bf[:], r_state[:]), "sG", 1)
        nl0d, nl0e, nl0g = C.cnt["sD"], C.cnt["sE"], C.cnt["sG"]

        if dbg:
            @blk.sync
            def _(sy):
                sy.wait_ge(C.sem["sG"], nl0g)
                C.inc(sy.dma_start(out=rdbg_t, in_=r_state[:]), "sD", 16)

        for c in range(NT_OWN):
            sl = slice(c * 128, (c + 1) * 128)
            sl1 = slice(HALO + c * 128, HALO + (c + 1) * 128)

            @blk.sync
            def _(sy, c=c, sl=sl, sl1=sl1):
                if c == 0:
                    sy.wait_ge(C.sem["sD"], nl0d)
                else:
                    C.w(sy, "sG")
                    C.w(sy, "sT")
                C.inc(sy.dma_start(out=yl[:], in_=y_hbm[sl, :]), "sD", 16)
                C.inc(sy.dma_start(out=zz[:], in_=z_hbm[sl, :]), "sD", 16)
                C.inc(sy.dma_start(out=x1t[:], in_=x1_hbm[sl1, :]), "sD", 16)
                C.inc(sy.dma_start(out=plt[:], in_=pln_hbm[sl1, :]), "sD", 16)
                C.inc(sy.dma_start(out=ccm2[:],
                                   in_=xbc_hbm[DIN + 128:DIN + 256, sl]),
                      "sD", 16)
            nld = C.cnt["sD"]

            @blk.tensor
            def _(t, c=c, nld=nld):
                t.wait_ge(C.sem["sD"], nld)
                if c == 0:
                    t.wait_ge(C.sem["sG"], nl0g)
                else:
                    C.w(t, "sV")
                t.ldweights(ccm2[:])
                for h in range(NH):
                    ii = t.matmul(m3ps[:, h * HD:(h + 1) * HD], ccm2[:],
                                  r_bf[:, h * HD:(h + 1) * HD],
                                  start=True, stop=True)
                C.inc(ii, "sT", 1)
            nyc = C.cnt["sT"]

            @blk.vector
            def _(v, c=c, nyc=nyc):
                v.wait_ge(C.sem["sT"], nyc)
                if c >= 1:
                    C.w(v, "sG")
                eap = eacd_tm[:, c * NH:(c + 1) * NH].unsqueeze(2) \
                    .broadcast_to([128, NH, HD])
                C.inc(v.tensor_tensor(
                    ycor[:].rearrange("p (h e) -> p h e", e=HD),
                    m3ps[:, 0:DIN].rearrange("p (h e) -> p h e", e=HD),
                    eap, ALU.mult), "sV", 1)
            nycv = C.cnt["sV"]

            @blk.scalar
            def _(s, nld=nld):
                s.wait_ge(C.sem["sD"], nld)
                C.inc(s.activation(sz[:], zz[:], AF.Silu), "sS", 1)
            nsz = C.cnt["sS"]

            @blk.gpsimd
            def _(g, nycv=nycv, nsz=nsz):
                g.wait_ge(C.sem["sV"], nycv)
                g.wait_ge(C.sem["sS"], nsz)
                g.tensor_add(y2[:], ycor[:], yl[:])
                C.inc(g.tensor_mul(yg[:], y2[:], sz[:]), "sG", 1)
            nyg = C.cnt["sG"]

            @blk.scalar
            def _(s, nyg=nyg):
                s.wait_ge(C.sem["sG"], nyg)
                C.inc(s.activation(m3ps[:, 2048:4096], yg[:], AF.Square,
                                   bias=0.0, scale=1.0,
                                   accum_out=stc[:, SQc:SQc + 1]), "sS", 1)

            @blk.scalar
            def _(s):
                s.mul(stc[:, T0:T0 + 1], stc[:, SQc:SQc + 1], 1.0 / DIN)
                C.inc(s.add(stc[:, T0:T0 + 1], stc[:, T0:T0 + 1], EPS),
                      "sS", 1)

            emit_rsqrt(blk, C, stc[:, T0:T0 + 1], stc[:, RS:RS + 1],
                       stc[:, T1:T1 + 1])

            @blk.scalar
            def _(s):
                C.inc(s.mul(tb[:], yg[:], stc[:, RS:RS + 1]), "sS", 1)

            if dbg:
                @blk.sync
                def _(sy, c=c, sl=sl):
                    C.w(sy, "sG")
                    C.inc(sy.dma_start(out=ydbg_t[sl, :], in_=y2[:]),
                          "sD", 16)

            @blk.vector
            def _(v):
                C.w(v, "sS")
                if True:
                    C.w(v, "sD")
                C.inc(v.scalar_tensor_tensor(ynb[:], tb[:], 1.0, gatew_sb[:],
                                             ALU.mult, ALU.mult), "sV", 1)

            if dbg:
                @blk.sync
                def _(sy, c=c, sl=sl):
                    C.w(sy, "sV")
                    C.inc(sy.dma_start(out=yndbg_t[sl, :], in_=ynb[:]),
                          "sD", 16)

            @blk.sync
            def _(sy, c=c):
                C.w(sy, "sV")
                if c >= 1:
                    C.w(sy, "sT")
                for k in range(16):
                    C.inc(sy.dma_start_transpose(
                        yn_fm[:, k * 128:(k + 1) * 128],
                        ynb[:, k * 128:(k + 1) * 128]), "sD", 16)
            ntr = C.cnt["sD"]

            @blk.tensor
            def _(t, c=c, ntr=ntr):
                t.wait_ge(C.sem["sD"], ntr)
                if c == 0:
                    t.wait_ge(C.sem["sE"], nl0e)
                C.w(t, "sV")
                if c >= 1:
                    C.w(t, "sS")
                for k in range(16):
                    lw = yn_fm[:, k * 128:(k + 1) * 128]
                    t.ldweights(lw)
                    for b in range(2):
                        ii = t.matmul(
                            m3ps[:, 2048 + b * 512:2048 + (b + 1) * 512],
                            lw, opj_bf[:, k * D + b * 512:k * D + (b + 1) * 512],
                            start=(k == 0), stop=(k == 15))
                C.inc(ii, "sT", 1)
            nop_ = C.cnt["sT"]

            @blk.scalar
            def _(s, nop_=nop_):
                s.wait_ge(C.sem["sT"], nop_)
                C.inc(s.copy(x2a[:], m3ps[:, 2048:3072]), "sS", 1)

            @blk.vector
            def _(v):
                C.w(v, "sS")
                C.inc(v.tensor_add(x2t[:], x2a[:], x1t[:]), "sV", 1)
            @blk.gpsimd
            def _(g, c=c):
                C.w(g, "sV")
                C.inc(g.tensor_add(x2t[:], x2t[:], plt[:]), "sG", 1)

            @blk.sync
            def _(sy, c=c, sl=sl):
                C.w(sy, "sG")
                C.inc(sy.dma_start(out=x2_hbm[sl, :], in_=x2t[:]), "sD", 16)


def stage_inputs(inputs):
    f32 = np.float32
    x = np.ascontiguousarray(
        np.asarray(inputs["L0_attn_hidden_states"], f32).reshape(2 * 16384, D))
    rep = lambda v, n=128: np.ascontiguousarray(
        np.broadcast_to(np.asarray(v, f32).reshape(1, -1), (n, np.asarray(v).size)))
    w1 = np.asarray(inputs["L0_ffn_w1"], f32)
    w2 = np.asarray(inputs["L0_ffn_w2"], f32)
    conv_w = np.asarray(inputs["L0_attn_conv_w"], f32).reshape(CONVD, 4)
    convw_r = np.ascontiguousarray(
        conv_w.reshape(18, 128, 4).transpose(1, 0, 2).reshape(128, 72))
    convb_r = np.ascontiguousarray(
        np.asarray(inputs["L0_attn_conv_b"], f32).reshape(18, 128).T)

    shared = {
        "w1t": np.ascontiguousarray(w1.T),
        "w2t": np.ascontiguousarray(w2.T),
        "inpj": np.ascontiguousarray(np.asarray(inputs["L0_attn_in_proj_w"], f32)),
        "outpj": np.ascontiguousarray(np.asarray(inputs["L0_attn_out_proj_w"], f32)),
        "ln1g_r": rep(inputs["L0_ffn_ln1_g"]),
        "ln1b_r": rep(inputs["L0_ffn_ln1_b"]),
        "ln2g_r": rep(inputs["L0_ffn_ln2_g"]),
        "ln2b_r": rep(inputs["L0_ffn_ln2_b"]),
        "b1_r": rep(inputs["L0_ffn_b1"]),
        "b2_r": rep(inputs["L0_ffn_b2"]),
        "nrmw_r": rep(inputs["L0_attn_norm_weight"]),
        "gatew_r": rep(inputs["L0_attn_gate_norm_w"]),
        "dtbias_r": rep(inputs["L0_attn_dt_bias"]),
        "alog_r": rep(inputs["L0_attn_A_log"]),
        "dskip_r": rep(inputs["L0_attn_D_skip"]),
        "convw_r": convw_r,
        "convb_r": convb_r,
    }
    in_maps = []
    for c in range(NCORES):
        m = dict(shared)
        slab = np.zeros((T_SLAB, D), f32)
        lo = c * T_OWN
        slab[HALO:] = x[lo:lo + T_OWN]
        seq_start = (c % 4 == 0)
        if not seq_start:
            slab[:HALO] = x[lo - HALO:lo]
        m["x0"] = slab
        m["hm_r"] = np.full((128, 1), 0.0 if seq_start else 1.0, f32)
        mask = np.zeros((128, 8), f32)
        sel = np.zeros((128, 64), f32)
        s0 = (c // 4) * 4
        for j in range(s0, c):
            mask[:, j] = 1.0
            for mm in range(j + 1, c):
                sel[:, j * 8 + mm] = 1.0
        m["mask_r"] = mask
        m["sel_r"] = sel
        in_maps.append(m)
    return in_maps


def kernel(**inputs):
    in_maps = stage_inputs(inputs)
    nc = build_nc()
    res = run_bass_kernel_spmd(nc, in_maps, core_ids=list(range(NCORES)))
    out = np.concatenate([res.results[c]["x3"] for c in range(NCORES)], axis=0)
    return out.reshape(2, 16384, D).astype(np.float32)
